# version 29 (final)
# baseline (speedup 1.0000x reference)
import numpy as np
import contextlib
import concourse.bass as bass
import concourse.mybir as mybir
from concourse.bass_utils import run_bass_kernel_spmd

F32 = mybir.dt.float32
BF16 = mybir.dt.bfloat16
U8 = mybir.dt.uint8
AF = mybir.ActivationFunctionType
ALU = mybir.AluOpType
AX = mybir.AxisListType

NCORES = 8
T = 1024
TC = 8
D = 4096
DC = 32
NH = 32
DQ = 128
OSL = 256
NSL = D // OSL
GD = 8
NG = DQ // GD
SCALE = float(1.0 / np.sqrt(DQ))
EPS = 1e-5
LORA_SCALE = 2.0
NSM = 5

NS_LN = TC

NV_CONST = 1
NV_F01 = 3
NV_QROW = 4
NV_ROWS = 5
NV_WFOLD = 11
NV_PREP = 12

NT_QKV = TC
NT_LORA = NT_QKV + 1
def t_ffn(idx):
    return NT_LORA + idx + 1

NTILE = NSL * TC
NSLOT = 16


def plan_v4():
    NCG = TC * NG
    att = [("negc", 0), ("sc", 0), ("sc", 1), ("sc", 2), ("sc", 3)]
    for cg in range(NCG):
        att.append(("dn", cg))
        if cg >= 2:
            att.append(("orr", cg - 2))
            if (cg - 2) % NG == NG - 1:
                att.append(("div", (cg - 2) // NG))
        nxt = cg + 4
        if nxt < NCG:
            if nxt % NG == 0:
                att.append(("negc", nxt // NG))
            att.append(("sc", nxt))
    att.append(("orr", NCG - 2))
    att.append(("orr", NCG - 1))
    att.append(("div", TC - 1))

    T0 = 150.0
    QP = 10.8
    H0 = T0 + 8 * QP + 10.0
    FT = 5.5
    evac_est = [T0 + QP * (c + 1) + 0.5 for c in range(TC)]
    hev_est = [H0 + 1.0, H0 + 2.0]
    cost = {"negc": 0.5, "sc": 1.25, "dn": 1.25, "orr": 1.25, "div": 1.2}
    att_est = []
    clk = evac_est[0] + 1.0
    for ev in att:
        clk += cost[ev[0]]
        att_est.append(clk)
    sc_est = {ev[1]: e for ev, e in zip(att, att_est) if ev[0] == "sc"}
    exp_est = {cg: sc_est[cg] + 1.5 for cg in range(NCG)}
    div_est = {ev[1]: e for ev, e in zip(att, att_est) if ev[0] == "div"}
    copy_est = [H0 + 8.0 + FT * i for i in range(NTILE)]
    add_est = [copy_est[i] + 0.8 for i in range(NTILE)]
    JCUT = {}
    for c in range(TC):
        lim = div_est[c] * 1.15 + 15.0
        j = NSL
        while j > 1 and add_est[(j - 1) * TC + c] + 1.5 >= lim:
            j -= 1
        JCUT[c] = j
    wiss_est = {1: T0 + 8 * QP + 2.0}
    for j in range(2, NSL):
        wiss_est[j] = H0 + 8.0 + FT * TC * (j - 1) - 18.0

    SCORESV, ORV, DIVV = {}, {}, {}
    v = NV_PREP
    for ev in att:
        if ev[0] == "sc":
            v += 1
            SCORESV[ev[1]] = v
        elif ev[0] == "orr":
            v += 1
            ORV[ev[1]] = v
        elif ev[0] == "div":
            v += 1
            DIVV[ev[1]] = v

    g_events = []
    pi = ai = 0
    wlist = sorted(range(1, NSL), key=lambda j: wiss_est[j])
    wi = 0
    while pi < NCG or ai < NTILE or wi < len(wlist):
        cands = []
        if pi < NCG:
            cands.append((exp_est[pi] + 0.5, 0, "prod"))
        if ai < NTILE:
            cands.append((add_est[ai], 1, "add"))
        if wi < len(wlist):
            j = wlist[wi]
            need_add = max(0, TC * (j - 1) - 1 - 24)
            if ai > need_add or ai >= NTILE:
                cands.append((wiss_est[j], 2, "wiss"))
        if not cands:
            cands.append((add_est[ai], 1, "add"))
        _, _, kind = min(cands)
        if kind == "prod":
            g_events.append(("prod", pi)); pi += 1
        elif kind == "add":
            g_events.append(("add", ai)); ai += 1
        else:
            g_events.append(("wiss", wlist[wi])); wi += 1
    PRODG, ADDG = {}, {}
    gv = 0
    for ev in g_events:
        if ev[0] in ("prod", "add"):
            gv += 1
            if ev[0] == "prod":
                PRODG[ev[1]] = gv
            else:
                ADDG[ev[1]] = gv

    addq = {}
    best = -1
    for ev in g_events:
        if ev[0] == "prod":
            best = ev[1]
        elif ev[0] == "add":
            addq[ev[1]] = best

    s_events = []
    vi = hi = ei = ci = 0
    while vi < TC or hi < 2 or ei < NCG or ci < NTILE:
        cands = []
        if vi < TC:
            cands.append((evac_est[vi], 0, "evac"))
        if hi < 2 and vi == TC:
            cands.append((hev_est[hi], 1, "hev"))
        if ei < NCG and vi > ei // NG:
            cands.append((exp_est[ei], 2, "exp"))
        need_exp = addq.get(max(0, ci - 16), -1) if ci < NTILE else -1
        if ci < NTILE and hi == 2 and (ci < 16 or ei > need_exp):
            cands.append((copy_est[ci], 3, "copy"))
        _, _, kind = min(cands)
        if kind == "evac":
            s_events.append(("evac", vi)); vi += 1
        elif kind == "hev":
            s_events.append(("hev", hi)); hi += 1
        elif kind == "exp":
            s_events.append(("exp", ei)); ei += 1
        else:
            s_events.append(("copy", ci)); ci += 1
    EXPS, SCOPY, SEVAC = {}, {}, {}
    SH = None
    sv = NS_LN
    for ev in s_events:
        sv += 1
        if ev[0] == "exp":
            EXPS[ev[1]] = sv
        elif ev[0] == "copy":
            SCOPY[ev[1]] = sv
        elif ev[0] == "evac":
            SEVAC[ev[1]] = sv
        else:
            SH = sv
    return (att, s_events, g_events, SCORESV, ORV, DIVV, EXPS, SCOPY,
            PRODG, ADDG, SEVAC, SH, JCUT)


(V_EVENTS, S_EVENTS, G_EVENTS, SCORESV, ORV, DIVV, EXPS, SCOPY,
 PRODG, ADDG, SEVAC, SH, JCUT) = plan_v4()
VADD = {c: DIVV[TC - 1] + c + 1 for c in range(TC)}


def build_nc():
    nc = bass.Bass("TRN2", target_bir_lowering=False, debug=False)

    x_h = nc.dram_tensor("x_sh", [T, D], F32, kind="ExternalInput").ap()
    wq_h = nc.dram_tensor("wq_sl", [512, DQ, NH], F32, kind="ExternalInput").ap()
    cc_in = nc.dram_tensor("cc_in", [512, DQ], BF16)
    cc_out = nc.dram_tensor("cc_out", [D, DQ], BF16)
    wk_h = nc.dram_tensor("wk_dd", [D, DQ], F32, kind="ExternalInput").ap()
    wv_h = nc.dram_tensor("wv_dd", [D, DQ], F32, kind="ExternalInput").ap()
    wt_h = nc.dram_tensor("wt_do", [D, D], F32, kind="ExternalInput").ap()
    at_h = nc.dram_tensor("at_dr", [D, 128], F32, kind="ExternalInput").ap()
    bt_h = nc.dram_tensor("bt_ro", [128, D], F32, kind="ExternalInput").ap()
    bq_h = nc.dram_tensor("bq_dh", [DQ, NH], F32, kind="ExternalInput").ap()
    bk_h = nc.dram_tensor("bk_f", [DQ], F32, kind="ExternalInput").ap()
    bv_h = nc.dram_tensor("bv_f", [DQ], F32, kind="ExternalInput").ap()
    fb_h = nc.dram_tensor("fb_f", [D], F32, kind="ExternalInput").ap()
    lw_h = nc.dram_tensor("lw_rep", [128], F32, kind="ExternalInput").ap()
    out_h = nc.dram_tensor("out_sh", [T, D], F32, kind="ExternalOutput").ap()

    es = contextlib.ExitStack()
    xnT_t = es.enter_context(nc.sbuf_tensor("xnT", [128, DC * T], BF16))
    regB = es.enter_context(nc.sbuf_tensor("regB", [128, 24576], U8))
    wtb0_t = es.enter_context(nc.sbuf_tensor("wtb0", [128, DC * OSL], BF16))
    regC = es.enter_context(nc.sbuf_tensor("regC", [128, 32768], U8))
    regD = es.enter_context(nc.sbuf_tensor("regD", [128, 16384], U8))
    regE = es.enter_context(nc.sbuf_tensor("regE", [128, 16384], U8))
    btb_t = es.enter_context(nc.sbuf_tensor("btb", [128, D], BF16))
    hb_t = es.enter_context(nc.sbuf_tensor("hb", [128, T], BF16))
    qs_t = es.enter_context(nc.sbuf_tensor("qs", [128, T], F32))
    kk_t = es.enter_context(nc.sbuf_tensor("kk", [128, T], F32))
    vv_t = es.enter_context(nc.sbuf_tensor("vv", [128, T], BF16))
    oat_t = es.enter_context(nc.sbuf_tensor("oat", [128, T], F32))
    dnb_t = es.enter_context(nc.sbuf_tensor("dnb", [128, 2, DQ], F32))
    orb_t = es.enter_context(nc.sbuf_tensor("orb", [128, 2, DQ], F32))
    stats_t = es.enter_context(nc.sbuf_tensor("stats", [128, TC, 8], F32))
    ext_t = es.enter_context(nc.sbuf_tensor("ext", [128, 8], F32))
    negc_t = es.enter_context(nc.sbuf_tensor("negc", [128, TC], F32))
    coef_t = es.enter_context(nc.sbuf_tensor("coef", [128, 1], F32))
    ones2_t = es.enter_context(nc.sbuf_tensor("ones2", [2, 128], BF16))
    cpool_t = es.enter_context(nc.sbuf_tensor("cpool", [128, 6], F32))
    wqslb_t = es.enter_context(nc.sbuf_tensor("wqslb", [128, 4, DQ], BF16))
    rows2_t = es.enter_context(nc.sbuf_tensor("rows2", [2, 640], F32))
    qrow_t = es.enter_context(nc.sbuf_tensor("qrow", [2, 384], BF16))
    brow_t = es.enter_context(nc.sbuf_tensor("brow", [2, D], BF16))
    psq_t = es.enter_context(nc.psum_tensor("psq", [128, 2, 512], F32))
    psl_t = es.enter_context(nc.psum_tensor("psl", [128, 2, 512], F32))
    psf_t = es.enter_context(nc.psum_tensor("psf", [128, 4, 512], F32))
    dX = es.enter_context(nc.semaphore("dX"))
    dT = es.enter_context(nc.semaphore("dT"))
    dXR = es.enter_context(nc.semaphore("dXR"))
    dST = es.enter_context(nc.semaphore("dST"))
    dRow = es.enter_context(nc.semaphore("dRow"))
    dW = es.enter_context(nc.semaphore("dW"))
    dSm = es.enter_context(nc.semaphore("dSm"))
    dB = es.enter_context(nc.semaphore("dB"))
    dG = es.enter_context(nc.semaphore("dG"))
    dR4 = [es.enter_context(nc.semaphore(f"dR{i}")) for i in range(4)]
    dSTG = es.enter_context(nc.semaphore("dSTG"))
    dBp = [es.enter_context(nc.semaphore(f"dBp{i}")) for i in range(2)]
    ccs = es.enter_context(nc.semaphore("ccs"))
    sG = es.enter_context(nc.semaphore("sG"))
    sS = es.enter_context(nc.semaphore("sS"))
    sV = es.enter_context(nc.semaphore("sV"))
    sT = es.enter_context(nc.semaphore("sT"))

    with es, nc.Block() as block:
        cpool = cpool_t[:]
        _CONSTS = [EPS, -0.5, 1.0 / D, -1.0, 1.0, 0.0]
        for _i, _cv in enumerate(_CONSTS):
            nc.const_aps.register(F32, _cv, cpool[:, _i:_i + 1])

        xnT = xnT_t[:].rearrange("p (dc t) -> p dc t", t=T)
        wqkv = regB[:].bitcast(BF16).rearrange("p (dc f) -> p dc f", f=384)
        wtb1 = regB[:, 0:16384].bitcast(BF16).rearrange("p (dc f) -> p dc f", f=OSL)
        wtb0 = wtb0_t[:].rearrange("p (dc f) -> p dc f", f=OSL)
        wtb = [wtb0, wtb1]
        xld = regC[:].bitcast(F32).rearrange("p (b f) -> p b f", b=2)
        xre = regC[:, 0:16384].bitcast(F32).rearrange("p (b c f) -> p b c f", b=2, c=TC)
        oslot = regC[:, 16384:32768].bitcast(F32).rearrange("p (s f) -> p s f", s=NSLOT)
        rowA2 = regB[:, 0:16384].bitcast(F32)
        rows2 = rows2_t[:]
        rbufs = [regC[:, 0:16384].bitcast(F32), regD[:].bitcast(F32),
                 regC[:, 16384:32768].bitcast(F32), regE[:].bitcast(F32)]
        wfold8 = regD[:].bitcast(F32).rearrange("p (b d h) -> p b d h", b=2, h=NH)
        wqslb = wqslb_t[:]
        scb = regD[:].bitcast(F32).rearrange("p (b g e) -> p b g e", b=4, g=GD)
        xnb = regE[:, 0:8192].bitcast(BF16)
        atb = regC[:, 16384:24576].bitcast(BF16).rearrange("p (dc r) -> p dc r", r=128)
        epb = regE[:, 0:8192].bitcast(BF16).rearrange("p (b g e) -> p b g e", b=4, g=GD)
        evb = regE[:, 8192:16384].bitcast(BF16).rearrange("p (b g e) -> p b g e", b=4, g=GD)
        qs, kk, vv, oat = qs_t[:], kk_t[:], vv_t[:], oat_t[:]
        hb, btb = hb_t[:], btb_t[:]
        dnb, orb = dnb_t[:], orb_t[:]
        ext, negc, coef = ext_t[:], negc_t[:], coef_t[:]
        ones2 = ones2_t[:]
        qrow, brow = qrow_t[:], brow_t[:]
        psq, psl, psf = psq_t[:], psl_t[:], psf_t[:]

        def pslot(idx):
            s = idx % 8
            if s < 2:
                return psq[:, s, 0:OSL]
            if s < 4:
                return psl[:, s - 2, 0:OSL]
            return psf[:, s - 4, 0:OSL]

        @block.sync
        def _(sync):
            for r in range(2):
                sync.dma_start(out=rowA2[r:r + 1, 0:4096], in_=bq_h[:, :]).then_inc(dRow, 16)
                sync.dma_start(out=rows2[r:r + 1, 0:128],
                               in_=bk_h.rearrange("(a f) -> a f", a=1)).then_inc(dRow, 16)
                sync.dma_start(out=rows2[r:r + 1, 128:256],
                               in_=bv_h.rearrange("(a f) -> a f", a=1)).then_inc(dRow, 16)
            sync.wait_ge(sV, NV_QROW)
            for r in range(2):
                sync.dma_start(out=rowA2[r:r + 1, 0:4096],
                               in_=fb_h.rearrange("(a f) -> a f", a=1)).then_inc(dRow, 16)
            for c in range(TC):
                sync.wait_ge(sS, c + 1)
                if c >= 1:
                    sync.wait_ge(dT, 16 * c)
                sync.dma_start(
                    out=xnT[:, :, c * 128:(c + 1) * 128],
                    in_=xnb.rearrange("p (dc f) -> p dc f", f=128),
                    transpose=True,
                ).then_inc(dT, 16)
                if c + 2 < TC:
                    sync.wait_ge(dX, 16 * (c + 2))
                    sync.dma_start(out=xld[:, c % 2, :],
                                   in_=x_h[(c + 2) * 128:(c + 3) * 128, :]).then_inc(dX, 16)
            sync.wait_ge(sS, NS_LN)
            for j in range(2):
                if j >= 1:
                    sync.wait_ge(dXR, 16 * j)
                sync.dma_start(
                    out=xre[:, j % 2, :, :],
                    in_=x_h.rearrange("(c p) o -> p c o", p=128)[:, :, j * OSL:(j + 1) * OSL],
                ).then_inc(dXR, 16)
            for idx in range(NTILE):
                j, c = idx // TC, idx % TC
                if idx % TC == 0 and j + 2 < NSL:
                    jj = j + 2
                    sync.wait_ge(sG, ADDG[(jj - 1) * TC - 1])
                    sync.wait_ge(dXR, 16 * jj)
                    sync.dma_start(
                        out=xre[:, jj % 2, :, :],
                        in_=x_h.rearrange("(c p) o -> p c o", p=128)[:, :, jj * OSL:(jj + 1) * OSL],
                    ).then_inc(dXR, 16)
                sync.wait_ge(sG, ADDG[idx])
                if idx >= 1:
                    sync.wait_ge(dST, 16 * idx)
                sync.dma_start(
                    out=out_h[c * 128:(c + 1) * 128, j * OSL:(j + 1) * OSL],
                    in_=oslot[:, idx % NSLOT, :],
                ).then_inc(dST, 16)
            sync.wait_ge(dST, 16 * NTILE)
            def rd(c):
                w = JCUT[c] * OSL
                sync.dma_start(out=rbufs[c % 4][:, 0:w],
                               in_=out_h[c * 128:(c + 1) * 128, 0:w]).then_inc(dR4[c % 4], 16)
            def wr(c):
                w = JCUT[c] * OSL
                sync.wait_ge(sV, VADD[c])
                sync.dma_start(out=out_h[c * 128:(c + 1) * 128, 0:w],
                               in_=rbufs[c % 4][:, 0:w]).then_inc(dST, 16)
            rd(0); rd(2)
            wr(0)
            sync.wait_ge(dST, 16 * (NTILE + 1)); rd(4)
            wr(2)
            sync.wait_ge(dST, 16 * (NTILE + 2)); rd(6)
            wr(4)
            wr(6)
            sync.wait_ge(dST, 16 * (NTILE + 4))
            sync.wait_ge(dSTG, 16 * 4)

        @block.gpsimd
        def _(g):
            g.dma_start(out=coef[:, 0:1], in_=lw_h.rearrange("(p a) -> p a", a=1)).then_inc(dSm, 16)
            g.dma_start(out=xld[:, 0, :], in_=x_h[0:128, :]).then_inc(dX, 16)
            EV8 = [2, 3, 6, 7, 8, 9]
            for e in range(8):
                if e >= 2:
                    g.wait_ge(sV, EV8[e - 2])
                q, dh = e // 2, e % 2
                g.dma_start(out=wfold8[:, e % 2, :, :],
                            in_=wq_h[128 * q:128 * (q + 1), 64 * dh:64 * (dh + 1), :]
                            ).then_inc(dBp[e % 2], 16)
                if e == 1:
                    g.wait_ge(dX, 16)
                    g.dma_start(out=xld[:, 1, :], in_=x_h[128:256, :]).then_inc(dX, 16)
            g.wait_ge(sV, NV_ROWS)
            g.wait_ge(dSm, 16)
            g.dma_start(out=wqkv[:, :, 128:256],
                        in_=wk_h.rearrange("(dc p) d -> p dc d", p=128)).then_inc(dSm, 16)
            g.wait_ge(dSm, 32)
            g.dma_start(out=wqkv[:, :, 256:384],
                        in_=wv_h.rearrange("(dc p) d -> p dc d", p=128)).then_inc(dSm, 16)
            g.wait_ge(sV, NV_WFOLD)
            g.dma_start(out=cc_in.ap().rearrange("(dcl p) d -> p dcl d", p=128),
                        in_=wqslb[:, :, :]).then_inc(dB, 16)
            g.wait_ge(dB, 16)
            g.collective_compute(
                "AllGather", ALU.bypass,
                replica_groups=[list(range(NCORES))],
                ins=[cc_in.ap().opt()],
                outs=[cc_out.ap().opt()],
            ).then_inc(ccs, 1)
            g.wait_ge(ccs, 1)
            g.dma_start(out=wqkv[:, :, 0:128],
                        in_=cc_out.ap().rearrange("(dc p) d -> p dc d", p=128)).then_inc(dG, 16)
            g.wait_ge(dSm, 48)
            g.wait_ge(sS, NS_LN)
            g.dma_start(out=atb[:, :, :], in_=at_h.rearrange("(dc p) r -> p dc r", p=128)).then_inc(dSm, 16)
            g.wait_ge(dSm, 64)
            g.dma_start(out=btb, in_=bt_h[:, :]).then_inc(dSm, 16)
            def wiss(j):
                if j == 1:
                    g.wait_ge(sT, NT_QKV)
                if j >= 2:
                    g.wait_ge(sT, t_ffn((j - 1) * TC - 1))
                if j >= 1:
                    g.wait_ge(dW, 16 * j)
                g.dma_start(
                    out=wtb[j % 2][:, :, :],
                    in_=wt_h.rearrange("(dc p) o -> p dc o", p=128)[:, :, j * OSL:(j + 1) * OSL],
                ).then_inc(dW, 16)

            wiss(0)
            for ev in G_EVENTS:
                if ev[0] == "wiss":
                    wiss(ev[1])
                elif ev[0] == "prod":
                    cg = ev[1]
                    c, gi = divmod(cg, NG)
                    if cg >= 4:
                        g.wait_ge(sV, ORV[cg - 4])
                    g.wait_ge(sS, EXPS[cg])
                    g.tensor_tensor(
                        evb[:, cg % 4, :, :], epb[:, cg % 4, :, :],
                        vv[:, c * 128:(c + 1) * 128].unsqueeze(1).broadcast_to([128, GD, 128]),
                        ALU.mult).then_inc(sG, 1)
                else:
                    idx = ev[1]
                    j, c = idx // TC, idx % TC
                    g.wait_ge(sS, SCOPY[idx])
                    g.wait_ge(dXR, 16 * (j + 1))
                    if j >= JCUT[c]:
                        g.tensor_tensor(oslot[:, idx % NSLOT, :], oslot[:, idx % NSLOT, :],
                                        xre[:, j % 2, c, :], ALU.add)
                        g.wait_ge(sV, DIVV[c])
                        ot = oslot[:, idx % NSLOT, :].rearrange("p (d h) -> p d h", h=NH)
                        g.tensor_tensor(
                            ot, ot,
                            oat[:, c * 128 + j * GD:c * 128 + (j + 1) * GD]
                                .unsqueeze(2).broadcast_to([128, GD, NH]),
                            ALU.add).then_inc(sG, 1)
                    else:
                        g.tensor_tensor(oslot[:, idx % NSLOT, :], oslot[:, idx % NSLOT, :],
                                        xre[:, j % 2, c, :], ALU.add).then_inc(sG, 1)
            g.wait_ge(sV, DIVV[TC - 1])
            g.wait_ge(dST, 16 * NTILE)
            def grd(c):
                w = JCUT[c] * OSL
                g.dma_start(out=rbufs[c % 4][:, 0:w],
                            in_=out_h[c * 128:(c + 1) * 128, 0:w]).then_inc(dR4[c % 4], 16)
            def gwr(c):
                w = JCUT[c] * OSL
                g.wait_ge(sV, VADD[c])
                g.dma_start(out=out_h[c * 128:(c + 1) * 128, 0:w],
                            in_=rbufs[c % 4][:, 0:w]).then_inc(dSTG, 16)
            grd(1); grd(3)
            gwr(1)
            g.wait_ge(dSTG, 16); grd(5)
            gwr(3)
            g.wait_ge(dSTG, 32); grd(7)
            gwr(5)
            gwr(7)

        @block.scalar
        def _(sc):
            for c in range(TC):
                sc.wait_ge(dX, 16 * (c + 1))
                if c == 0:
                    sc.wait_ge(sV, NV_CONST)
                if c >= 1:
                    sc.wait_ge(dT, 16 * c)
                xt = xld[:, c % 2, :]
                st = stats_t[:, c, :]
                sc.activation(out=xnb, in_=xt, func=AF.Square, accum_out=st[:, 1:2])
                sc.activation(out=xnb, in_=xt, func=AF.Copy, accum_out=st[:, 0:1])
                sc.mul(st[:, 2:3], st[:, 0:1], 1.0 / D)
                sc.activation(out=st[:, 3:4], in_=st[:, 2:3], func=AF.Square)
                sc.mul(st[:, 3:4], st[:, 3:4], -1.0)
                sc.mul(st[:, 4:5], st[:, 1:2], 1.0 / D)
                sc.add(st[:, 4:5], st[:, 4:5], st[:, 3:4])
                sc.activation(out=st[:, 5:6], in_=st[:, 4:5], func=AF.Ln, bias=EPS, scale=1.0)
                sc.activation(out=st[:, 5:6], in_=st[:, 5:6], func=AF.Exp, scale=-0.5)
                sc.mul(st[:, 6:7], st[:, 2:3], -1.0)
                sc.mul(st[:, 6:7], st[:, 6:7], st[:, 5:6])
                sc.activation(out=xnb, in_=xt, func=AF.Identity,
                              bias=st[:, 6:7], scale=st[:, 5:6]).then_inc(sS, 1)
            first_exp = True
            for ev in S_EVENTS:
                if ev[0] == "exp" and first_exp:
                    sc.wait_ge(dT, 16 * TC)
                    first_exp = False
                if ev[0] == "evac":
                    c = ev[1]
                    sc.wait_ge(sT, c + 1)
                    pq = psq[:, c % 2, :]
                    sc.copy(qs[:, c * 128:(c + 1) * 128], pq[:, 0:128])
                    sc.copy(kk[:, c * 128:(c + 1) * 128], pq[:, 128:256])
                    sc.copy(vv[:, c * 128:(c + 1) * 128], pq[:, 256:384]).then_inc(sS, 1)
                elif ev[0] == "hev":
                    hh = ev[1]
                    sc.wait_ge(sT, NT_LORA)
                    if hh == 0:
                        sc.wait_ge(sV, NV_PREP)
                    sc.activation(out=hb[:, hh * 512:(hh + 1) * 512], in_=psl[:, hh, :],
                                  func=AF.Identity, scale=coef[:, 0:1]).then_inc(sS, 1)
                elif ev[0] == "exp":
                    cg = ev[1]
                    c, gi = divmod(cg, NG)
                    sc.wait_ge(sV, SCORESV[cg])
                    if cg >= 4:
                        sc.wait_ge(sG, PRODG[cg - 4])
                    sc.activation(out=epb[:, cg % 4, :, :], in_=scb[:, cg % 4, :, :],
                                  func=AF.Exp, bias=negc[:, c:c + 1], scale=1.0).then_inc(sS, 1)
                else:
                    idx = ev[1]
                    if idx >= NSLOT:
                        sc.wait_ge(dST, 16 * (idx - NSLOT + 1))
                    sc.wait_ge(sT, t_ffn(idx))
                    sc.copy(oslot[:, idx % NSLOT, :], pslot(idx)).then_inc(sS, 1)

        @block.tensor
        def _(t):
            for c in range(TC):
                t.wait_ge(dT, 16 * (c + 1))
                if c == 0:
                    t.wait_ge(dG, 16)
                    t.wait_ge(dSm, 16 * 3)
                if c >= 2:
                    t.wait_ge(sS, SEVAC[c - 2])
                pq = psq[:, c % 2, 0:384]
                for dc in range(DC):
                    t.matmul(pq, xnT[:, dc, c * 128:(c + 1) * 128], wqkv[:, dc, :],
                             start=(dc == 0), stop=False)
                if c == 0:
                    t.wait_ge(sV, NV_PREP)
                t.matmul(pq, ones2, qrow, start=False, stop=True).then_inc(sT, 1)
            t.wait_ge(dSm, 16 * 4)
            ins = None
            for hh in range(2):
                for dc in range(DC):
                    ins = t.matmul(psl[:, hh, :], atb[:, dc, :],
                                   xnT[:, dc, hh * 512:(hh + 1) * 512],
                                   start=(dc == 0), stop=(dc == DC - 1))
            ins.then_inc(sT, 1)
            for j in range(NSL):
                t.wait_ge(dW, 16 * (j + 1))
                for c in range(TC):
                    idx = j * TC + c
                    if idx == 0:
                        t.wait_ge(sS, SH)
                        t.wait_ge(dSm, 16 * 5)
                    if idx >= 8:
                        t.wait_ge(sS, SCOPY[idx - 8])
                    pf = pslot(idx)
                    for dc in range(DC):
                        t.matmul(pf, xnT[:, dc, c * 128:(c + 1) * 128],
                                 wtb[j % 2][:, dc, :], start=(dc == 0), stop=False)
                    t.matmul(pf, ones2, brow[:, j * OSL:(j + 1) * OSL], start=False, stop=False)
                    t.matmul(pf, hb[:, c * 128:(c + 1) * 128], btb[:, j * OSL:(j + 1) * OSL],
                             start=False, stop=True).then_inc(sT, 1)

        @block.vector
        def _(v):
            ins = None
            for _i, _cv in enumerate(_CONSTS):
                ins = v.memset(cpool[:, _i:_i + 1], _cv)
            ins.then_inc(sV, 1)
            for e in range(2):
                v.wait_ge(dBp[e % 2], 16 * (e // 2 + 1))
                with nc.allow_low_precision(reason="reduce accumulates fp32; bf16 is the output cast"):
                    v.tensor_reduce(wqslb[:, e // 2, 64 * (e % 2):64 * (e % 2) + 64],
                                    wfold8[:, e % 2, :, :], AX.X, ALU.add).then_inc(sV, 1)
            v.wait_ge(dRow, 16 * 6)
            qrf = rows2[:, 256:640]
            v.tensor_reduce(qrf[:, 0:128],
                            rowA2[0:2, 0:4096].rearrange("r (d h) -> r d h", h=NH),
                            AX.X, ALU.add)
            v.tensor_copy(qrf[:, 128:384], rows2[:, 0:256])
            v.tensor_copy(qrow, qrf)
            v.tensor_sub(qrf, qrf, qrow)
            v.tensor_copy(qrow[0:1, :], qrf[0:1, :]).then_inc(sV, 1)
            v.wait_ge(dRow, 16 * 8)
            v.tensor_copy(brow, rowA2[0:2, 0:4096])
            v.tensor_sub(rowA2[0:2, 0:4096], rowA2[0:2, 0:4096], brow)
            v.tensor_copy(brow[0:1, :], rowA2[0:1, 0:4096]).then_inc(sV, 1)
            for e in range(2, 8):
                v.wait_ge(dBp[e % 2], 16 * (e // 2 + 1))
                with nc.allow_low_precision(reason="reduce accumulates fp32; bf16 is the output cast"):
                    v.tensor_reduce(wqslb[:, e // 2, 64 * (e % 2):64 * (e % 2) + 64],
                                    wfold8[:, e % 2, :, :], AX.X, ALU.add).then_inc(sV, 1)
            v.wait_ge(dSm, 16)
            v.memset(ones2, 1.0)
            v.tensor_scalar(coef[:, 0:1], coef[:, 0:1], LORA_SCALE, None, ALU.mult).then_inc(sV, 1)
            def do_negc(c):
                v.wait_ge(sS, SEVAC[c])
                qc = qs[:, c * 128:(c + 1) * 128]
                kc = kk[:, c * 128:(c + 1) * 128]
                v.tensor_reduce(ext[:, 0:1], qc, AX.X, ALU.max)
                v.tensor_reduce(ext[:, 1:2], qc, AX.X, ALU.min)
                v.tensor_reduce(ext[:, 2:3], kc, AX.X, ALU.max)
                v.tensor_reduce(ext[:, 3:4], kc, AX.X, ALU.min)
                v.tensor_mul(ext[:, 4:5], ext[:, 0:1], ext[:, 2:3])
                v.tensor_mul(ext[:, 5:6], ext[:, 0:1], ext[:, 3:4])
                v.tensor_max(ext[:, 4:5], ext[:, 4:5], ext[:, 5:6])
                v.tensor_mul(ext[:, 5:6], ext[:, 1:2], ext[:, 2:3])
                v.tensor_max(ext[:, 4:5], ext[:, 4:5], ext[:, 5:6])
                v.tensor_mul(ext[:, 5:6], ext[:, 1:2], ext[:, 3:4])
                v.tensor_max(ext[:, 4:5], ext[:, 4:5], ext[:, 5:6])
                v.tensor_scalar(negc[:, c:c + 1], ext[:, 4:5], -SCALE, None, ALU.mult)

            def do_sc(cg):
                c, gi = divmod(cg, NG)
                if cg >= 4:
                    v.wait_ge(sS, EXPS[cg - 4])
                v.scalar_tensor_tensor(
                    scb[:, cg % 4, :, :],
                    qs[:, c * 128 + gi * GD:c * 128 + (gi + 1) * GD]
                        .unsqueeze(2).broadcast_to([128, GD, 128]),
                    SCALE,
                    kk[:, c * 128:(c + 1) * 128].unsqueeze(1).broadcast_to([128, GD, 128]),
                    ALU.mult, ALU.mult).then_inc(sV, 1)

            def do_dn(cg):
                c, gi = divmod(cg, NG)
                v.wait_ge(sS, EXPS[cg])
                v.tensor_reduce(dnb[:, c % 2, gi * GD:(gi + 1) * GD],
                                epb[:, cg % 4, :, :], AX.X, ALU.add)

            def do_orr(cg):
                c, gi = divmod(cg, NG)
                v.wait_ge(sG, PRODG[cg])
                v.tensor_reduce(orb[:, c % 2, gi * GD:(gi + 1) * GD],
                                evb[:, cg % 4, :, :], AX.X, ALU.add).then_inc(sV, 1)

            def do_div(c):
                v.reciprocal(dnb[:, c % 2, :], dnb[:, c % 2, :])
                v.tensor_mul(oat[:, c * 128:(c + 1) * 128], orb[:, c % 2, :],
                             dnb[:, c % 2, :]).then_inc(sV, 1)

            for ev in V_EVENTS:
                kind, arg = ev
                if kind == "negc":
                    do_negc(arg)
                elif kind == "sc":
                    do_sc(arg)
                elif kind == "dn":
                    do_dn(arg)
                elif kind == "orr":
                    do_orr(arg)
                else:
                    do_div(arg)
            for c in range(TC):
                nd = JCUT[c] * GD
                v.wait_ge(dR4[c % 4], 16 * (c // 4 + 1))
                rb3 = rbufs[c % 4][:, 0:nd * NH].rearrange("p (d h) -> p d h", h=NH)
                v.tensor_tensor(
                    rb3, rb3,
                    oat[:, c * 128:c * 128 + nd].unsqueeze(2).broadcast_to([128, nd, NH]),
                    ALU.add).then_inc(sV, 1)

    return nc


_NC_CACHE = []


def kernel(L0_attn_x, L0_attn_Wq, L0_attn_bq, L0_attn_Wk, L0_attn_bk,
           L0_attn_Wv, L0_attn_bv, L0_ffn_lora_A, L0_ffn_lora_B, L0_ffn_W,
           L0_ffn_bias, L0_ffn_lora_weights):
    x = np.ascontiguousarray(np.asarray(L0_attn_x, dtype=np.float32).reshape(8192, 4096))
    wq_full = np.asarray(L0_attn_Wq, dtype=np.float32)
    wk = np.ascontiguousarray(np.asarray(L0_attn_Wk, dtype=np.float32)[:, :, 0])
    wv = np.ascontiguousarray(np.asarray(L0_attn_Wv, dtype=np.float32)[:, :, 0])
    wt = np.ascontiguousarray(np.asarray(L0_ffn_W, dtype=np.float32).T)
    at = np.ascontiguousarray(np.asarray(L0_ffn_lora_A, dtype=np.float32).reshape(128, 4096).T)
    bt = np.ascontiguousarray(np.asarray(L0_ffn_lora_B, dtype=np.float32).transpose(0, 2, 1).reshape(128, 4096))
    bq = np.ascontiguousarray(np.asarray(L0_attn_bq, dtype=np.float32))
    bkf = np.ascontiguousarray(np.asarray(L0_attn_bk, dtype=np.float32).reshape(-1))
    bvf = np.ascontiguousarray(np.asarray(L0_attn_bv, dtype=np.float32).reshape(-1))
    fb = np.ascontiguousarray(np.asarray(L0_ffn_bias, dtype=np.float32).reshape(-1))
    lw = np.ascontiguousarray(np.repeat(np.asarray(L0_ffn_lora_weights, dtype=np.float32), 16))

    if not _NC_CACHE:
        _NC_CACHE.append(build_nc())
    nc = _NC_CACHE[0]
    shards = np.split(x, NCORES, axis=0)
    in_maps = [{
        "x_sh": shards[i],
        "wq_sl": np.ascontiguousarray(wq_full[512 * i:512 * (i + 1)]),
        "wk_dd": wk, "wv_dd": wv,
        "wt_do": wt, "at_dr": at, "bt_ro": bt, "bq_dh": bq, "bk_f": bkf,
        "bv_f": bvf, "fb_f": fb, "lw_rep": lw,
    } for i in range(NCORES)]
    res = run_bass_kernel_spmd(nc, in_maps, core_ids=list(range(NCORES)))
    out = np.concatenate([r["out_sh"] for r in res.results], axis=0)
    return out.reshape(1, 8192, 4096)


# version 34
# speedup vs baseline: 1.0153x; 1.0153x over previous
import numpy as np
import contextlib
import concourse.bass as bass
import concourse.mybir as mybir
from concourse.bass_utils import run_bass_kernel_spmd

F32 = mybir.dt.float32
BF16 = mybir.dt.bfloat16
U8 = mybir.dt.uint8
AF = mybir.ActivationFunctionType
ALU = mybir.AluOpType
AX = mybir.AxisListType

NCORES = 8
T = 1024
TC = 8
D = 4096
DC = 32
NH = 32
DQ = 128
OSL = 256
NSL = D // OSL
GD = 8
NG = DQ // GD
SCALE = float(1.0 / np.sqrt(DQ))
EPS = 1e-5
LORA_SCALE = 2.0
NSM = 5

NS_LN = TC

NV_CONST = 1
NV_F01 = 3
NV_QROW = 4
NV_ROWS = 5
NV_WFOLD = 11
NV_PREP = 12

NT_QKV = TC
NT_LORA = NT_QKV + 1
def t_ffn(idx):
    return NT_LORA + idx + 1

NTILE = NSL * TC
NSLOT = 16


def plan_v4():
    NCG = TC * NG
    att = [("negc", 0), ("sc", 0), ("sc", 1), ("sc", 2), ("sc", 3)]
    for cg in range(NCG):
        att.append(("dn", cg))
        if cg >= 2:
            att.append(("orr", cg - 2))
            if (cg - 2) % NG == NG - 1:
                att.append(("div", (cg - 2) // NG))
        nxt = cg + 4
        if nxt < NCG:
            if nxt % NG == 0:
                att.append(("negc", nxt // NG))
            att.append(("sc", nxt))
    att.append(("orr", NCG - 2))
    att.append(("orr", NCG - 1))
    att.append(("div", TC - 1))

    T0 = 150.0
    QP = 10.8
    H0 = T0 + 8 * QP + 10.0
    FT = 5.5
    evac_est = [T0 + QP * (c + 1) + 0.5 for c in range(TC)]
    hev_est = [H0 + 1.0, H0 + 2.0]
    cost = {"negc": 0.5, "sc": 1.25, "dn": 1.25, "orr": 1.25, "div": 1.2}
    att_est = []
    clk = evac_est[0] + 1.0
    for ev in att:
        clk += cost[ev[0]]
        att_est.append(clk)
    sc_est = {ev[1]: e for ev, e in zip(att, att_est) if ev[0] == "sc"}
    exp_est = {cg: sc_est[cg] + 1.5 for cg in range(NCG)}
    div_est = {ev[1]: e for ev, e in zip(att, att_est) if ev[0] == "div"}
    copy_est = [H0 + 8.0 + FT * i for i in range(NTILE)]
    add_est = [copy_est[i] + 0.8 for i in range(NTILE)]
    JCUT = {}
    for c in range(TC):
        lim = div_est[c] * 1.15 + 15.0
        j = NSL
        while j > 1 and add_est[(j - 1) * TC + c] + 1.5 >= lim:
            j -= 1
        JCUT[c] = j
    wiss_est = {1: T0 + 8 * QP + 2.0}
    for j in range(2, NSL):
        wiss_est[j] = H0 + 8.0 + FT * TC * (j - 1) - 18.0

    SCORESV, ORV, DIVV = {}, {}, {}
    v = NV_PREP
    for ev in att:
        if ev[0] == "sc":
            v += 1
            SCORESV[ev[1]] = v
        elif ev[0] == "orr":
            v += 1
            ORV[ev[1]] = v
        elif ev[0] == "div":
            v += 1
            DIVV[ev[1]] = v

    g_events = []
    pi = ai = 0
    wlist = sorted(range(1, NSL), key=lambda j: wiss_est[j])
    wi = 0
    while pi < NCG or ai < NTILE or wi < len(wlist):
        cands = []
        if pi < NCG:
            cands.append((exp_est[pi] + 0.5, 0, "prod"))
        if ai < NTILE:
            cands.append((add_est[ai], 1, "add"))
        if wi < len(wlist):
            j = wlist[wi]
            need_add = max(0, TC * (j - 1) - 1 - 24)
            if ai > need_add or ai >= NTILE:
                cands.append((wiss_est[j], 2, "wiss"))
        if not cands:
            cands.append((add_est[ai], 1, "add"))
        _, _, kind = min(cands)
        if kind == "prod":
            g_events.append(("prod", pi)); pi += 1
        elif kind == "add":
            g_events.append(("add", ai)); ai += 1
        else:
            g_events.append(("wiss", wlist[wi])); wi += 1
    PRODG, ADDG = {}, {}
    gv = 0
    for ev in g_events:
        if ev[0] in ("prod", "add"):
            gv += 1
            if ev[0] == "prod":
                PRODG[ev[1]] = gv
            else:
                ADDG[ev[1]] = gv

    addq = {}
    best = -1
    for ev in g_events:
        if ev[0] == "prod":
            best = ev[1]
        elif ev[0] == "add":
            addq[ev[1]] = best

    s_events = []
    vi = hi = ei = ci = 0
    while vi < TC or hi < 2 or ei < NCG or ci < NTILE:
        cands = []
        if vi < TC:
            cands.append((evac_est[vi], 0, "evac"))
        if hi < 2 and vi == TC:
            cands.append((hev_est[hi], 1, "hev"))
        if ei < NCG and vi > ei // NG:
            cands.append((exp_est[ei], 2, "exp"))
        need_exp = addq.get(max(0, ci - 16), -1) if ci < NTILE else -1
        if ci < NTILE and hi == 2 and (ci < 16 or ei > need_exp):
            cands.append((copy_est[ci], 3, "copy"))
        _, _, kind = min(cands)
        if kind == "evac":
            s_events.append(("evac", vi)); vi += 1
        elif kind == "hev":
            s_events.append(("hev", hi)); hi += 1
        elif kind == "exp":
            s_events.append(("exp", ei)); ei += 1
        else:
            s_events.append(("copy", ci)); ci += 1
    EXPS, SCOPY, SEVAC = {}, {}, {}
    SH = None
    sv = NS_LN
    for ev in s_events:
        sv += 1
        if ev[0] == "exp":
            EXPS[ev[1]] = sv
        elif ev[0] == "copy":
            SCOPY[ev[1]] = sv
        elif ev[0] == "evac":
            SEVAC[ev[1]] = sv
        else:
            SH = sv
    return (att, s_events, g_events, SCORESV, ORV, DIVV, EXPS, SCOPY,
            PRODG, ADDG, SEVAC, SH, JCUT)


(V_EVENTS, S_EVENTS, G_EVENTS, SCORESV, ORV, DIVV, EXPS, SCOPY,
 PRODG, ADDG, SEVAC, SH, JCUT) = plan_v4()
VADD = {c: DIVV[TC - 1] + c + 1 for c in range(TC)}


def build_nc():
    nc = bass.Bass("TRN2", target_bir_lowering=False, debug=False)

    x_h = nc.dram_tensor("x_sh", [T, D], F32, kind="ExternalInput").ap()
    wq_h = nc.dram_tensor("wq_sl", [512, DQ, NH], F32, kind="ExternalInput").ap()
    cc_in = nc.dram_tensor("cc_in", [512, DQ], BF16)
    cc_out = nc.dram_tensor("cc_out", [D, DQ], BF16)
    wk_h = nc.dram_tensor("wk_dd", [D, DQ], F32, kind="ExternalInput").ap()
    wv_h = nc.dram_tensor("wv_dd", [D, DQ], F32, kind="ExternalInput").ap()
    wt_h = nc.dram_tensor("wt_do", [D, D], F32, kind="ExternalInput").ap()
    at_h = nc.dram_tensor("at_dr", [D, 128], F32, kind="ExternalInput").ap()
    bt_h = nc.dram_tensor("bt_ro", [128, D], F32, kind="ExternalInput").ap()
    bq_h = nc.dram_tensor("bq_dh", [DQ, NH], F32, kind="ExternalInput").ap()
    bk_h = nc.dram_tensor("bk_f", [DQ], F32, kind="ExternalInput").ap()
    bv_h = nc.dram_tensor("bv_f", [DQ], F32, kind="ExternalInput").ap()
    fb_h = nc.dram_tensor("fb_f", [D], F32, kind="ExternalInput").ap()
    lw_h = nc.dram_tensor("lw_rep", [128], F32, kind="ExternalInput").ap()
    out_h = nc.dram_tensor("out_sh", [T, D], F32, kind="ExternalOutput").ap()

    es = contextlib.ExitStack()
    xnT_t = es.enter_context(nc.sbuf_tensor("xnT", [128, DC * T], BF16))
    regB = es.enter_context(nc.sbuf_tensor("regB", [128, 24576], U8))
    wtb0_t = es.enter_context(nc.sbuf_tensor("wtb0", [128, DC * OSL], BF16))
    regC = es.enter_context(nc.sbuf_tensor("regC", [128, 32768], U8))
    regD = es.enter_context(nc.sbuf_tensor("regD", [128, 16384], U8))
    regE = es.enter_context(nc.sbuf_tensor("regE", [128, 16384], U8))
    btb_t = es.enter_context(nc.sbuf_tensor("btb", [128, D], BF16))
    hb_t = es.enter_context(nc.sbuf_tensor("hb", [128, T], BF16))
    qs_t = es.enter_context(nc.sbuf_tensor("qs", [128, T], F32))
    kk_t = es.enter_context(nc.sbuf_tensor("kk", [128, T], F32))
    vv_t = es.enter_context(nc.sbuf_tensor("vv", [128, T], BF16))
    oat_t = es.enter_context(nc.sbuf_tensor("oat", [128, T], F32))
    dnb_t = es.enter_context(nc.sbuf_tensor("dnb", [128, 2, DQ], F32))
    orb_t = es.enter_context(nc.sbuf_tensor("orb", [128, 2, DQ], F32))
    stats_t = es.enter_context(nc.sbuf_tensor("stats", [128, TC, 8], F32))
    ext_t = es.enter_context(nc.sbuf_tensor("ext", [128, 8], F32))
    negc_t = es.enter_context(nc.sbuf_tensor("negc", [128, TC], F32))
    coef_t = es.enter_context(nc.sbuf_tensor("coef", [128, 1], F32))
    ones2_t = es.enter_context(nc.sbuf_tensor("ones2", [2, 128], BF16))
    cpool_t = es.enter_context(nc.sbuf_tensor("cpool", [128, 6], F32))
    wqslb_t = es.enter_context(nc.sbuf_tensor("wqslb", [128, 4, DQ], BF16))
    rows2_t = es.enter_context(nc.sbuf_tensor("rows2", [2, 640], F32))
    qrow_t = es.enter_context(nc.sbuf_tensor("qrow", [2, 384], BF16))
    brow_t = es.enter_context(nc.sbuf_tensor("brow", [2, D], BF16))
    psq_t = es.enter_context(nc.psum_tensor("psq", [128, 2, 512], F32))
    psl_t = es.enter_context(nc.psum_tensor("psl", [128, 2, 512], F32))
    psf_t = es.enter_context(nc.psum_tensor("psf", [128, 4, 512], F32))
    dX = es.enter_context(nc.semaphore("dX"))
    dT = es.enter_context(nc.semaphore("dT"))
    dXR = es.enter_context(nc.semaphore("dXR"))
    dST = es.enter_context(nc.semaphore("dST"))
    dRow = es.enter_context(nc.semaphore("dRow"))
    dW = es.enter_context(nc.semaphore("dW"))
    dSm = es.enter_context(nc.semaphore("dSm"))
    dB = es.enter_context(nc.semaphore("dB"))
    dG = es.enter_context(nc.semaphore("dG"))
    dR4 = [es.enter_context(nc.semaphore(f"dR{i}")) for i in range(4)]
    dSTG = es.enter_context(nc.semaphore("dSTG"))
    dW1 = es.enter_context(nc.semaphore("dW1"))
    dBp = [es.enter_context(nc.semaphore(f"dBp{i}")) for i in range(2)]
    ccs = es.enter_context(nc.semaphore("ccs"))
    sG = es.enter_context(nc.semaphore("sG"))
    sS = es.enter_context(nc.semaphore("sS"))
    sV = es.enter_context(nc.semaphore("sV"))
    sT = es.enter_context(nc.semaphore("sT"))

    with es, nc.Block() as block:
        cpool = cpool_t[:]
        _CONSTS = [EPS, -0.5, 1.0 / D, -1.0, 1.0, 0.0]
        for _i, _cv in enumerate(_CONSTS):
            nc.const_aps.register(F32, _cv, cpool[:, _i:_i + 1])

        xnT = xnT_t[:].rearrange("p (dc t) -> p dc t", t=T)
        wqkv = regB[:].bitcast(BF16).rearrange("p (dc f) -> p dc f", f=384)
        wtb1 = regB[:, 0:16384].bitcast(BF16).rearrange("p (dc f) -> p dc f", f=OSL)
        wtb0 = wtb0_t[:].rearrange("p (dc f) -> p dc f", f=OSL)
        wtb = [wtb0, wtb1]
        xld = regC[:].bitcast(F32).rearrange("p (b f) -> p b f", b=2)
        xre = regC[:, 0:16384].bitcast(F32).rearrange("p (b c f) -> p b c f", b=2, c=TC)
        oslot = regC[:, 16384:32768].bitcast(F32).rearrange("p (s f) -> p s f", s=NSLOT)
        rowA2 = regB[:, 0:16384].bitcast(F32)
        rows2 = rows2_t[:]
        rbufs = [regC[:, 0:16384].bitcast(F32), regD[:].bitcast(F32),
                 regC[:, 16384:32768].bitcast(F32), regE[:].bitcast(F32)]
        wfold8 = regD[:].bitcast(F32).rearrange("p (b d h) -> p b d h", b=2, h=NH)
        wqslb = wqslb_t[:]
        scb = regD[:].bitcast(F32).rearrange("p (b g e) -> p b g e", b=4, g=GD)
        xnb = regE[:, 0:8192].bitcast(BF16)
        atb = regC[:, 16384:24576].bitcast(BF16).rearrange("p (dc r) -> p dc r", r=128)
        epb = regE[:, 0:8192].bitcast(BF16).rearrange("p (b g e) -> p b g e", b=4, g=GD)
        evb = regE[:, 8192:16384].bitcast(BF16).rearrange("p (b g e) -> p b g e", b=4, g=GD)
        qs, kk, vv, oat = qs_t[:], kk_t[:], vv_t[:], oat_t[:]
        hb, btb = hb_t[:], btb_t[:]
        dnb, orb = dnb_t[:], orb_t[:]
        ext, negc, coef = ext_t[:], negc_t[:], coef_t[:]
        ones2 = ones2_t[:]
        qrow, brow = qrow_t[:], brow_t[:]
        psq, psl, psf = psq_t[:], psl_t[:], psf_t[:]

        def pslot(idx):
            s = idx % 8
            if s < 2:
                return psq[:, s, 0:OSL]
            if s < 4:
                return psl[:, s - 2, 0:OSL]
            return psf[:, s - 4, 0:OSL]

        @block.sync
        def _(sync):
            for r in range(2):
                sync.dma_start(out=rowA2[r:r + 1, 0:4096], in_=bq_h[:, :]).then_inc(dRow, 16)
                sync.dma_start(out=rows2[r:r + 1, 0:128],
                               in_=bk_h.rearrange("(a f) -> a f", a=1)).then_inc(dRow, 16)
                sync.dma_start(out=rows2[r:r + 1, 128:256],
                               in_=bv_h.rearrange("(a f) -> a f", a=1)).then_inc(dRow, 16)
            sync.wait_ge(sV, NV_QROW)
            for r in range(2):
                sync.dma_start(out=rowA2[r:r + 1, 0:4096],
                               in_=fb_h.rearrange("(a f) -> a f", a=1)).then_inc(dRow, 16)
            for c in range(TC):
                sync.wait_ge(sS, c + 1)
                if c >= 1:
                    sync.wait_ge(dT, 16 * c)
                sync.dma_start(
                    out=xnT[:, :, c * 128:(c + 1) * 128],
                    in_=xnb.rearrange("p (dc f) -> p dc f", f=128),
                    transpose=True,
                ).then_inc(dT, 16)
                if c + 2 < TC:
                    sync.wait_ge(dX, 16 * (c + 2))
                    sync.dma_start(out=xld[:, c % 2, :],
                                   in_=x_h[(c + 2) * 128:(c + 3) * 128, :]).then_inc(dX, 16)
            sync.wait_ge(sS, NS_LN)
            for j in range(2):
                if j >= 1:
                    sync.wait_ge(dXR, 16 * j)
                sync.dma_start(
                    out=xre[:, j % 2, :, :],
                    in_=x_h.rearrange("(c p) o -> p c o", p=128)[:, :, j * OSL:(j + 1) * OSL],
                ).then_inc(dXR, 16)
            for idx in range(NTILE):
                j, c = idx // TC, idx % TC
                if idx % TC == 0 and j + 2 < NSL:
                    jj = j + 2
                    sync.wait_ge(sG, ADDG[(jj - 1) * TC - 1])
                    sync.wait_ge(dXR, 16 * jj)
                    sync.dma_start(
                        out=xre[:, jj % 2, :, :],
                        in_=x_h.rearrange("(c p) o -> p c o", p=128)[:, :, jj * OSL:(jj + 1) * OSL],
                    ).then_inc(dXR, 16)
                sync.wait_ge(sG, ADDG[idx])
                if idx >= 1:
                    sync.wait_ge(dST, 16 * idx)
                sync.dma_start(
                    out=out_h[c * 128:(c + 1) * 128, j * OSL:(j + 1) * OSL],
                    in_=oslot[:, idx % NSLOT, :],
                ).then_inc(dST, 16)
            sync.wait_ge(dST, 16 * NTILE)
            def rd(c):
                w = JCUT[c] * OSL
                sync.dma_start(out=rbufs[c % 4][:, 0:w],
                               in_=out_h[c * 128:(c + 1) * 128, 0:w]).then_inc(dR4[c % 4], 16)
            def wr(c):
                w = JCUT[c] * OSL
                sync.wait_ge(sV, VADD[c])
                sync.dma_start(out=out_h[c * 128:(c + 1) * 128, 0:w],
                               in_=rbufs[c % 4][:, 0:w]).then_inc(dST, 16)
            rd(0); rd(2)
            wr(0)
            sync.wait_ge(dST, 16 * (NTILE + 1)); rd(4)
            wr(2)
            sync.wait_ge(dST, 16 * (NTILE + 2)); rd(6)
            wr(4)
            wr(6)
            sync.wait_ge(dST, 16 * (NTILE + 4))
            sync.wait_ge(dSTG, 16 * 4)

        @block.gpsimd
        def _(g):
            g.dma_start(out=coef[:, 0:1], in_=lw_h.rearrange("(p a) -> p a", a=1)).then_inc(dSm, 16)
            g.dma_start(out=xld[:, 0, :], in_=x_h[0:128, :]).then_inc(dX, 16)
            EV8 = [2, 3, 6, 7, 8, 9]
            for e in range(8):
                if e >= 2:
                    g.wait_ge(sV, EV8[e - 2])
                q, dh = e // 2, e % 2
                g.dma_start(out=wfold8[:, e % 2, :, :],
                            in_=wq_h[128 * q:128 * (q + 1), 64 * dh:64 * (dh + 1), :]
                            ).then_inc(dBp[e % 2], 16)
                if e == 1:
                    g.wait_ge(dX, 16)
                    g.dma_start(out=xld[:, 1, :], in_=x_h[128:256, :]).then_inc(dX, 16)
            g.wait_ge(sV, NV_ROWS)
            g.wait_ge(dSm, 16)
            g.dma_start(out=wqkv[:, :, 128:256],
                        in_=wk_h.rearrange("(dc p) d -> p dc d", p=128)).then_inc(dSm, 16)
            g.wait_ge(dSm, 32)
            g.dma_start(out=wqkv[:, :, 256:384],
                        in_=wv_h.rearrange("(dc p) d -> p dc d", p=128)).then_inc(dSm, 16)
            g.wait_ge(sV, NV_WFOLD)
            g.dma_start(out=cc_in.ap().rearrange("(dcl p) d -> p dcl d", p=128),
                        in_=wqslb[:, :, :]).then_inc(dB, 16)
            g.wait_ge(dB, 16)
            g.collective_compute(
                "AllGather", ALU.bypass,
                replica_groups=[list(range(NCORES))],
                ins=[cc_in.ap().opt()],
                outs=[cc_out.ap().opt()],
            ).then_inc(ccs, 1)
            g.wait_ge(ccs, 1)
            g.dma_start(out=wqkv[:, :, 0:128],
                        in_=cc_out.ap().rearrange("(dc p) d -> p dc d", p=128)).then_inc(dG, 16)
            g.wait_ge(dSm, 48)
            g.wait_ge(sS, NS_LN)
            g.dma_start(out=atb[:, :, :], in_=at_h.rearrange("(dc p) r -> p dc r", p=128)).then_inc(dSm, 16)
            g.wait_ge(dSm, 64)
            g.dma_start(out=btb, in_=bt_h[:, :]).then_inc(dSm, 16)
            def wiss(j):
                if j == 1:
                    g.wait_ge(sT, NT_QKV)
                if j >= 2:
                    g.wait_ge(sT, t_ffn((j - 1) * TC - 1))
                src = wt_h.rearrange("(dc p) o -> p dc o", p=128)
                if j >= 1:
                    g.wait_ge(dW, 16 * j)
                g.dma_start(
                    out=wtb[j % 2][:, 0:16, :],
                    in_=src[:, 0:16, j * OSL:(j + 1) * OSL],
                ).then_inc(dW, 16)
                if j >= 1:
                    g.wait_ge(dW1, 16 * j)
                g.dma_start(
                    out=wtb[j % 2][:, 16:32, :],
                    in_=src[:, 16:32, j * OSL:(j + 1) * OSL],
                ).then_inc(dW1, 16)

            wiss(0)
            for ev in G_EVENTS:
                if ev[0] == "wiss":
                    wiss(ev[1])
                elif ev[0] == "prod":
                    cg = ev[1]
                    c, gi = divmod(cg, NG)
                    if cg >= 4:
                        g.wait_ge(sV, ORV[cg - 4])
                    g.wait_ge(sS, EXPS[cg])
                    g.tensor_tensor(
                        evb[:, cg % 4, :, :], epb[:, cg % 4, :, :],
                        vv[:, c * 128:(c + 1) * 128].unsqueeze(1).broadcast_to([128, GD, 128]),
                        ALU.mult).then_inc(sG, 1)
                else:
                    idx = ev[1]
                    j, c = idx // TC, idx % TC
                    g.wait_ge(sS, SCOPY[idx])
                    g.wait_ge(dXR, 16 * (j + 1))
                    if j >= JCUT[c]:
                        g.tensor_tensor(oslot[:, idx % NSLOT, :], oslot[:, idx % NSLOT, :],
                                        xre[:, j % 2, c, :], ALU.add)
                        g.wait_ge(sV, DIVV[c])
                        ot = oslot[:, idx % NSLOT, :].rearrange("p (d h) -> p d h", h=NH)
                        g.tensor_tensor(
                            ot, ot,
                            oat[:, c * 128 + j * GD:c * 128 + (j + 1) * GD]
                                .unsqueeze(2).broadcast_to([128, GD, NH]),
                            ALU.add).then_inc(sG, 1)
                    else:
                        g.tensor_tensor(oslot[:, idx % NSLOT, :], oslot[:, idx % NSLOT, :],
                                        xre[:, j % 2, c, :], ALU.add).then_inc(sG, 1)
            g.wait_ge(sV, DIVV[TC - 1])
            g.wait_ge(dST, 16 * NTILE)
            def grd(c):
                w = JCUT[c] * OSL
                g.dma_start(out=rbufs[c % 4][:, 0:w],
                            in_=out_h[c * 128:(c + 1) * 128, 0:w]).then_inc(dR4[c % 4], 16)
            def gwr(c):
                w = JCUT[c] * OSL
                g.wait_ge(sV, VADD[c])
                g.dma_start(out=out_h[c * 128:(c + 1) * 128, 0:w],
                            in_=rbufs[c % 4][:, 0:w]).then_inc(dSTG, 16)
            grd(1); grd(3)
            gwr(1)
            g.wait_ge(dSTG, 16); grd(5)
            gwr(3)
            g.wait_ge(dSTG, 32); grd(7)
            gwr(5)
            gwr(7)

        @block.scalar
        def _(sc):
            for c in range(TC):
                sc.wait_ge(dX, 16 * (c + 1))
                if c == 0:
                    sc.wait_ge(sV, NV_CONST)
                if c >= 1:
                    sc.wait_ge(dT, 16 * c)
                xt = xld[:, c % 2, :]
                st = stats_t[:, c, :]
                sc.activation(out=xnb, in_=xt, func=AF.Square, accum_out=st[:, 1:2])
                sc.activation(out=xnb, in_=xt, func=AF.Copy, accum_out=st[:, 0:1])
                sc.mul(st[:, 2:3], st[:, 0:1], 1.0 / D)
                sc.activation(out=st[:, 3:4], in_=st[:, 2:3], func=AF.Square)
                sc.mul(st[:, 3:4], st[:, 3:4], -1.0)
                sc.mul(st[:, 4:5], st[:, 1:2], 1.0 / D)
                sc.add(st[:, 4:5], st[:, 4:5], st[:, 3:4])
                sc.activation(out=st[:, 5:6], in_=st[:, 4:5], func=AF.Ln, bias=EPS, scale=1.0)
                sc.activation(out=st[:, 5:6], in_=st[:, 5:6], func=AF.Exp, scale=-0.5)
                sc.mul(st[:, 6:7], st[:, 2:3], -1.0)
                sc.mul(st[:, 6:7], st[:, 6:7], st[:, 5:6])
                sc.activation(out=xnb, in_=xt, func=AF.Identity,
                              bias=st[:, 6:7], scale=st[:, 5:6]).then_inc(sS, 1)
            first_exp = True
            for ev in S_EVENTS:
                if ev[0] == "exp" and first_exp:
                    sc.wait_ge(dT, 16 * TC)
                    first_exp = False
                if ev[0] == "evac":
                    c = ev[1]
                    sc.wait_ge(sT, c + 1)
                    pq = psq[:, c % 2, :]
                    sc.copy(qs[:, c * 128:(c + 1) * 128], pq[:, 0:128])
                    sc.copy(kk[:, c * 128:(c + 1) * 128], pq[:, 128:256])
                    sc.copy(vv[:, c * 128:(c + 1) * 128], pq[:, 256:384]).then_inc(sS, 1)
                elif ev[0] == "hev":
                    hh = ev[1]
                    sc.wait_ge(sT, NT_LORA)
                    if hh == 0:
                        sc.wait_ge(sV, NV_PREP)
                    sc.activation(out=hb[:, hh * 512:(hh + 1) * 512], in_=psl[:, hh, :],
                                  func=AF.Identity, scale=coef[:, 0:1]).then_inc(sS, 1)
                elif ev[0] == "exp":
                    cg = ev[1]
                    c, gi = divmod(cg, NG)
                    sc.wait_ge(sV, SCORESV[cg])
                    if cg >= 4:
                        sc.wait_ge(sG, PRODG[cg - 4])
                    sc.activation(out=epb[:, cg % 4, :, :], in_=scb[:, cg % 4, :, :],
                                  func=AF.Exp, bias=negc[:, c:c + 1], scale=1.0).then_inc(sS, 1)
                else:
                    idx = ev[1]
                    if idx >= NSLOT:
                        sc.wait_ge(dST, 16 * (idx - NSLOT + 1))
                    sc.wait_ge(sT, t_ffn(idx))
                    sc.copy(oslot[:, idx % NSLOT, :], pslot(idx)).then_inc(sS, 1)

        @block.tensor
        def _(t):
            for c in range(TC):
                t.wait_ge(dT, 16 * (c + 1))
                if c == 0:
                    t.wait_ge(dG, 16)
                    t.wait_ge(dSm, 16 * 3)
                if c >= 2:
                    t.wait_ge(sS, SEVAC[c - 2])
                pq = psq[:, c % 2, 0:384]
                for dc in range(DC):
                    t.matmul(pq, xnT[:, dc, c * 128:(c + 1) * 128], wqkv[:, dc, :],
                             start=(dc == 0), stop=False)
                if c == 0:
                    t.wait_ge(sV, NV_PREP)
                t.matmul(pq, ones2, qrow, start=False, stop=True).then_inc(sT, 1)
            t.wait_ge(dSm, 16 * 4)
            ins = None
            for hh in range(2):
                for dc in range(DC):
                    ins = t.matmul(psl[:, hh, :], atb[:, dc, :],
                                   xnT[:, dc, hh * 512:(hh + 1) * 512],
                                   start=(dc == 0), stop=(dc == DC - 1))
            ins.then_inc(sT, 1)
            for j in range(NSL):
                t.wait_ge(dW, 16 * (j + 1))
                for c in range(TC):
                    idx = j * TC + c
                    if idx == 0:
                        t.wait_ge(sS, SH)
                        t.wait_ge(dSm, 16 * 5)
                    if idx >= 8:
                        t.wait_ge(sS, SCOPY[idx - 8])
                    pf = pslot(idx)
                    for dc in range(DC):
                        if dc == 16 and c == 0:
                            t.wait_ge(dW1, 16 * (j + 1))
                        t.matmul(pf, xnT[:, dc, c * 128:(c + 1) * 128],
                                 wtb[j % 2][:, dc, :], start=(dc == 0), stop=False)
                    t.matmul(pf, ones2, brow[:, j * OSL:(j + 1) * OSL], start=False, stop=False)
                    t.matmul(pf, hb[:, c * 128:(c + 1) * 128], btb[:, j * OSL:(j + 1) * OSL],
                             start=False, stop=True).then_inc(sT, 1)

        @block.vector
        def _(v):
            ins = None
            for _i, _cv in enumerate(_CONSTS):
                ins = v.memset(cpool[:, _i:_i + 1], _cv)
            ins.then_inc(sV, 1)
            for e in range(2):
                v.wait_ge(dBp[e % 2], 16 * (e // 2 + 1))
                with nc.allow_low_precision(reason="reduce accumulates fp32; bf16 is the output cast"):
                    v.tensor_reduce(wqslb[:, e // 2, 64 * (e % 2):64 * (e % 2) + 64],
                                    wfold8[:, e % 2, :, :], AX.X, ALU.add).then_inc(sV, 1)
            v.wait_ge(dRow, 16 * 6)
            qrf = rows2[:, 256:640]
            v.tensor_reduce(qrf[:, 0:128],
                            rowA2[0:2, 0:4096].rearrange("r (d h) -> r d h", h=NH),
                            AX.X, ALU.add)
            v.tensor_copy(qrf[:, 128:384], rows2[:, 0:256])
            v.tensor_copy(qrow, qrf)
            v.tensor_sub(qrf, qrf, qrow)
            v.tensor_copy(qrow[0:1, :], qrf[0:1, :]).then_inc(sV, 1)
            v.wait_ge(dRow, 16 * 8)
            v.tensor_copy(brow, rowA2[0:2, 0:4096])
            v.tensor_sub(rowA2[0:2, 0:4096], rowA2[0:2, 0:4096], brow)
            v.tensor_copy(brow[0:1, :], rowA2[0:1, 0:4096]).then_inc(sV, 1)
            for e in range(2, 8):
                v.wait_ge(dBp[e % 2], 16 * (e // 2 + 1))
                with nc.allow_low_precision(reason="reduce accumulates fp32; bf16 is the output cast"):
                    v.tensor_reduce(wqslb[:, e // 2, 64 * (e % 2):64 * (e % 2) + 64],
                                    wfold8[:, e % 2, :, :], AX.X, ALU.add).then_inc(sV, 1)
            v.wait_ge(dSm, 16)
            v.memset(ones2, 1.0)
            v.tensor_scalar(coef[:, 0:1], coef[:, 0:1], LORA_SCALE, None, ALU.mult).then_inc(sV, 1)
            def do_negc(c):
                v.wait_ge(sS, SEVAC[c])
                qc = qs[:, c * 128:(c + 1) * 128]
                kc = kk[:, c * 128:(c + 1) * 128]
                v.tensor_reduce(ext[:, 0:1], qc, AX.X, ALU.max)
                v.tensor_reduce(ext[:, 1:2], qc, AX.X, ALU.min)
                v.tensor_reduce(ext[:, 2:3], kc, AX.X, ALU.max)
                v.tensor_reduce(ext[:, 3:4], kc, AX.X, ALU.min)
                v.tensor_mul(ext[:, 4:5], ext[:, 0:1], ext[:, 2:3])
                v.tensor_mul(ext[:, 5:6], ext[:, 0:1], ext[:, 3:4])
                v.tensor_max(ext[:, 4:5], ext[:, 4:5], ext[:, 5:6])
                v.tensor_mul(ext[:, 5:6], ext[:, 1:2], ext[:, 2:3])
                v.tensor_max(ext[:, 4:5], ext[:, 4:5], ext[:, 5:6])
                v.tensor_mul(ext[:, 5:6], ext[:, 1:2], ext[:, 3:4])
                v.tensor_max(ext[:, 4:5], ext[:, 4:5], ext[:, 5:6])
                v.tensor_scalar(negc[:, c:c + 1], ext[:, 4:5], -SCALE, None, ALU.mult)

            def do_sc(cg):
                c, gi = divmod(cg, NG)
                if cg >= 4:
                    v.wait_ge(sS, EXPS[cg - 4])
                v.scalar_tensor_tensor(
                    scb[:, cg % 4, :, :],
                    qs[:, c * 128 + gi * GD:c * 128 + (gi + 1) * GD]
                        .unsqueeze(2).broadcast_to([128, GD, 128]),
                    SCALE,
                    kk[:, c * 128:(c + 1) * 128].unsqueeze(1).broadcast_to([128, GD, 128]),
                    ALU.mult, ALU.mult).then_inc(sV, 1)

            def do_dn(cg):
                c, gi = divmod(cg, NG)
                v.wait_ge(sS, EXPS[cg])
                v.tensor_reduce(dnb[:, c % 2, gi * GD:(gi + 1) * GD],
                                epb[:, cg % 4, :, :], AX.X, ALU.add)

            def do_orr(cg):
                c, gi = divmod(cg, NG)
                v.wait_ge(sG, PRODG[cg])
                v.tensor_reduce(orb[:, c % 2, gi * GD:(gi + 1) * GD],
                                evb[:, cg % 4, :, :], AX.X, ALU.add).then_inc(sV, 1)

            def do_div(c):
                v.reciprocal(dnb[:, c % 2, :], dnb[:, c % 2, :])
                v.tensor_mul(oat[:, c * 128:(c + 1) * 128], orb[:, c % 2, :],
                             dnb[:, c % 2, :]).then_inc(sV, 1)

            for ev in V_EVENTS:
                kind, arg = ev
                if kind == "negc":
                    do_negc(arg)
                elif kind == "sc":
                    do_sc(arg)
                elif kind == "dn":
                    do_dn(arg)
                elif kind == "orr":
                    do_orr(arg)
                else:
                    do_div(arg)
            for c in range(TC):
                nd = JCUT[c] * GD
                v.wait_ge(dR4[c % 4], 16 * (c // 4 + 1))
                rb3 = rbufs[c % 4][:, 0:nd * NH].rearrange("p (d h) -> p d h", h=NH)
                v.tensor_tensor(
                    rb3, rb3,
                    oat[:, c * 128:c * 128 + nd].unsqueeze(2).broadcast_to([128, nd, NH]),
                    ALU.add).then_inc(sV, 1)

    return nc


_NC_CACHE = []


def kernel(L0_attn_x, L0_attn_Wq, L0_attn_bq, L0_attn_Wk, L0_attn_bk,
           L0_attn_Wv, L0_attn_bv, L0_ffn_lora_A, L0_ffn_lora_B, L0_ffn_W,
           L0_ffn_bias, L0_ffn_lora_weights):
    x = np.ascontiguousarray(np.asarray(L0_attn_x, dtype=np.float32).reshape(8192, 4096))
    wq_full = np.asarray(L0_attn_Wq, dtype=np.float32)
    wk = np.ascontiguousarray(np.asarray(L0_attn_Wk, dtype=np.float32)[:, :, 0])
    wv = np.ascontiguousarray(np.asarray(L0_attn_Wv, dtype=np.float32)[:, :, 0])
    wt = np.ascontiguousarray(np.asarray(L0_ffn_W, dtype=np.float32).T)
    at = np.ascontiguousarray(np.asarray(L0_ffn_lora_A, dtype=np.float32).reshape(128, 4096).T)
    bt = np.ascontiguousarray(np.asarray(L0_ffn_lora_B, dtype=np.float32).transpose(0, 2, 1).reshape(128, 4096))
    bq = np.ascontiguousarray(np.asarray(L0_attn_bq, dtype=np.float32))
    bkf = np.ascontiguousarray(np.asarray(L0_attn_bk, dtype=np.float32).reshape(-1))
    bvf = np.ascontiguousarray(np.asarray(L0_attn_bv, dtype=np.float32).reshape(-1))
    fb = np.ascontiguousarray(np.asarray(L0_ffn_bias, dtype=np.float32).reshape(-1))
    lw = np.ascontiguousarray(np.repeat(np.asarray(L0_ffn_lora_weights, dtype=np.float32), 16))

    if not _NC_CACHE:
        _NC_CACHE.append(build_nc())
    nc = _NC_CACHE[0]
    shards = np.split(x, NCORES, axis=0)
    in_maps = [{
        "x_sh": shards[i],
        "wq_sl": np.ascontiguousarray(wq_full[512 * i:512 * (i + 1)]),
        "wk_dd": wk, "wv_dd": wv,
        "wt_do": wt, "at_dr": at, "bt_ro": bt, "bq_dh": bq, "bk_f": bkf,
        "bv_f": bvf, "fb_f": fb, "lw_rep": lw,
    } for i in range(NCORES)]
    res = run_bass_kernel_spmd(nc, in_maps, core_ids=list(range(NCORES)))
    out = np.concatenate([r["out_sh"] for r in res.results], axis=0)
    return out.reshape(1, 8192, 4096)
